# version 02
# baseline (speedup 1.0000x reference)
import math
import os
import numpy as np

import concourse.bass as bass
import concourse.mybir as mybir
from concourse.bass_utils import run_bass_kernel_spmd

F32 = mybir.dt.float32
BF16 = mybir.dt.bfloat16
I32 = mybir.dt.int32
U32 = mybir.dt.uint32
AF = mybir.ActivationFunctionType
ALU = mybir.AluOpType
AXX = mybir.AxisListType.X

D = 1024
DQK = 128
S = 8192
B = 4
T = 64
NOWN = 32
EPS = 1e-5

SQ2PI = math.sqrt(0.5 * math.pi)
ALPHA = 1.0 / (S * SQ2PI)
GAMMA_L = -math.sqrt(0.5) / SQ2PI
TSCALE = math.sqrt(128.0) / SQ2PI
MASKVAL = -30000.0
MAGIC = 1597463007.0

OWN = {0: list(range(0, 16)) + list(range(48, 64)), 1: list(range(16, 48))}
PRE = {0: [None, 47], 1: [15, 31]}


def _bucket_table():
    n = np.arange(0, 512, dtype=np.int64)
    n_safe = np.maximum(n, 1).astype(np.float32)
    val = (np.log(n_safe / np.float32(16.0)) / np.float32(math.log(8.0))
           * np.float32(16.0)).astype(np.float32)
    large = 16 + val.astype(np.int32)
    large = np.minimum(large, 31)
    return np.where(n < 16, n.astype(np.int32), large)


_BUCKET = _bucket_table()


def _host_masks():
    fj = np.arange(128)[:, None]
    fi = np.arange(128)[None, :]
    n0 = fi - fj
    n1 = 128 + fi - fj
    base_diag = np.where(n0 < 0, MASKVAL, GAMMA_L).astype(np.float32)
    masks, pairs = [], []
    for pat, nn in ((0, n0), (1, n1)):
        valid = nn >= 0
        bb = _BUCKET[np.clip(nn, 0, 511)]
        for b in range(32):
            m = (valid & (bb == b)).astype(np.float32)
            if m.any():
                masks.append(m)
                pairs.append((pat, b))
    return np.ascontiguousarray(np.stack(masks)), pairs, np.ascontiguousarray(base_diag)


_MASKS, _MASK_PAIRS, _BASE_DIAG = _host_masks()
NMASK = len(_MASK_PAIRS)


class Sem:
    def __init__(self, name):
        self.name = name
        self.h = None
        self.n = 0


class Builder:

    def __init__(self):
        self.mode = "count"
        self.milestones = {}
        self.sems = {}
        self.eng = None

    def sem(self, name):
        if name not in self.sems:
            self.sems[name] = Sem(name)
        return self.sems[name]

    def reset(self):
        for s in self.sems.values():
            s.n = 0

    def label(self, sem, name):
        if self.mode == "count":
            self.milestones[name] = self.sems[sem].n
        else:
            assert self.milestones[name] == self.sems[sem].n, (
                name, self.milestones[name], self.sems[sem].n)

    def at(self, name):
        if self.mode == "emit":
            return self.milestones[name]
        return self.milestones.get(name, 0)

    def wait(self, sem, value):
        if self.mode == "emit" and value > 0:
            self.eng.wait_ge(self.sems[sem].h, value)

    def inc(self, ins, sem, k=1):
        s = self.sems[sem]
        s.n += k
        if self.mode == "emit":
            ins.then_inc(s.h, k)
        return s.n

    def seminc(self, sem, k):
        s = self.sems[sem]
        s.n += k
        if self.mode == "emit" and k > 0:
            self.eng.sem_inc(s.h, k)
        return s.n

    def drain(self):
        if self.mode == "emit":
            self.eng.drain()

    def nopinc(self, sem, k=1):
        s = self.sems[sem]
        s.n += k
        if self.mode == "emit":
            self.eng.nop().then_inc(s.h, k)
        return s.n


def build_program():
    nc = bass.Bass("TRN2", target_bir_lowering=False)

    x_d = nc.dram_tensor("x", [S, D], F32, kind="ExternalInput")
    xo_d = nc.dram_tensor("xo", [NOWN * 128, D], F32, kind="ExternalInput")
    xpre_d = nc.dram_tensor("xpre", [2, 128, D], F32, kind="ExternalInput")
    wqk_d = nc.dram_tensor("wqk", [D, DQK], F32, kind="ExternalInput")
    bqk_d = nc.dram_tensor("bqk", [1, DQK], F32, kind="ExternalInput")
    gb_d = nc.dram_tensor("gb", [128, 4], F32, kind="ExternalInput")
    wv_d = nc.dram_tensor("wv", [D, D], F32, kind="ExternalInput")
    bv_d = nc.dram_tensor("bv", [1, D], F32, kind="ExternalInput")
    tbl_d = nc.dram_tensor("tbl", [1, 32], F32, kind="ExternalInput")
    wrT_d = nc.dram_tensor("wrT", [D, D], F32, kind="ExternalInput")
    wkT_d = nc.dram_tensor("wkT", [D, D], F32, kind="ExternalInput")
    wvT_d = nc.dram_tensor("wvT", [D, D], F32, kind="ExternalInput")
    mur_d = nc.dram_tensor("mur", [128, 8], F32, kind="ExternalInput")
    muk_d = nc.dram_tensor("muk", [128, 8], F32, kind="ExternalInput")
    ident_d = nc.dram_tensor("ident", [128, 128], F32, kind="ExternalInput")
    masks_d = nc.dram_tensor("masks", [NMASK, 128, 128], F32, kind="ExternalInput")
    bdiag_d = nc.dram_tensor("bdiag", [128, 128], F32, kind="ExternalInput")

    out_d = nc.dram_tensor("out", [128, NOWN, D], F32, kind="ExternalOutput")
    qt_scr = nc.dram_tensor("qt_scr", [128, T, 128], BF16, kind="Internal")
    ao_scr = nc.dram_tensor("ao_scr", [128, NOWN, D], F32, kind="Internal")
    pid_d = nc.partition_id_tensor

    bld = Builder()
    for s in ["sX", "sQT", "sS", "sV", "sT", "sG", "b1",
              "qL", "qSp", "b2",
              "pX", "pA", "pW", "pS", "pV", "pT", "pO"]:
        bld.sem(s)

    from contextlib import ExitStack

    es0 = ExitStack()
    with es0:
        pid_sb = es0.enter_context(nc.sbuf_tensor("pid_sb", [1, 1], U32))
        cst = es0.enter_context(nc.sbuf_tensor("cst", [128, 8], F32))
        tbl_sb = es0.enter_context(nc.sbuf_tensor("tbl_sb", [1, 32], F32))
        tbl_bc = es0.enter_context(nc.sbuf_tensor("tbl_bc", [128, 32], F32))
        gb_sb = es0.enter_context(nc.sbuf_tensor("gb_sb", [128, 4], F32))
        Pd = es0.enter_context(nc.sbuf_tensor("Pd", [128, 128], F32))
        P1 = es0.enter_context(nc.sbuf_tensor("P1", [128, 128], F32))
        id_bf = es0.enter_context(nc.sbuf_tensor("id_bf", [128, 128], BF16))
        ones_bf = es0.enter_context(nc.sbuf_tensor("ones_bf", [1, 128], BF16))
        bqk_bf = es0.enter_context(nc.sbuf_tensor("bqk_bf", [1, DQK], BF16))
        bv_bf = es0.enter_context(nc.sbuf_tensor("bv_bf", [1, D], BF16))
        for s in bld.sems.values():
            s.h = es0.enter_context(nc.semaphore(s.name))

        def gen_all(mode):
            bld.mode = mode
            emit = mode == "emit"

            def ctx(es, f):
                return es.enter_context(f) if emit else None

            bld.reset()

            es12 = ExitStack()
            kT_sb = ctx(es12, nc.sbuf_tensor("kT_sb", [128, S], BF16))
            v_sb = ctx(es12, nc.sbuf_tensor("v_sb", [128, T, D], BF16))

            es1 = ExitStack()
            xs = ctx(es1, nc.sbuf_tensor("xs", [128, 2, D], F32))
            sqs = ctx(es1, nc.sbuf_tensor("sqs", [128, D], F32))
            stc = ctx(es1, nc.sbuf_tensor("stc", [128, 16], F32))
            xn_bf = ctx(es1, nc.sbuf_tensor("xn_bf", [128, D], BF16))
            xnT_bf = ctx(es1, nc.sbuf_tensor("xnT_bf", [128, 2, 8, 128], BF16))
            sig_v = ctx(es1, nc.sbuf_tensor("sig_v", [128, D], F32))
            sig_qk = ctx(es1, nc.sbuf_tensor("sig_qk", [128, DQK], F32))
            qk_bf = ctx(es1, nc.sbuf_tensor("qk_bf", [128, DQK], BF16))
            wqk_bf = ctx(es1, nc.sbuf_tensor("wqk_bf", [128, 8, DQK], BF16))
            wv_bf = ctx(es1, nc.sbuf_tensor("wv_bf", [128, 8, D], BF16))
            qt_st = ctx(es1, nc.sbuf_tensor("qt_st", [128, 4, 128], BF16))
            msk = ctx(es1, nc.sbuf_tensor("msk", [128, 2, 8, 128], F32))
            TRp = ctx(es1, nc.psum_tensor("TRp", [128, 1024], F32))
            Vp = ctx(es1, nc.psum_tensor("Vp", [128, 1024], F32))
            QKp = ctx(es1, nc.psum_tensor("QKp", [128, 128], F32))
            QTp = ctx(es1, nc.psum_tensor("QTp", [128, 128], F32))

            NMCH = 0 if os.environ.get('KB_NOMASK') else (NMASK + 7) // 8
            _T = int(os.environ.get('KB_T', str(T)))

            def vec_rsqrt(st, vcol, zcol, tcol):
                E = bld.eng
                vi = st.bitcast(I32)
                E.tensor_copy(st[:, tcol:tcol + 1], vi[:, vcol:vcol + 1])
                E.drain()
                E.tensor_scalar(st[:, tcol:tcol + 1], st[:, tcol:tcol + 1],
                                -0.5, MAGIC, ALU.mult, ALU.add)
                E.drain()
                E.tensor_copy(vi[:, zcol:zcol + 1], st[:, tcol:tcol + 1])
                E.drain()
                for _ in range(3):
                    E.tensor_mul(st[:, tcol:tcol + 1], st[:, zcol:zcol + 1],
                                 st[:, zcol:zcol + 1])
                    E.drain()
                    E.tensor_mul(st[:, tcol:tcol + 1], st[:, tcol:tcol + 1],
                                 st[:, vcol:vcol + 1])
                    E.drain()
                    E.tensor_scalar(st[:, tcol:tcol + 1], st[:, tcol:tcol + 1],
                                    -0.5, 1.5, ALU.mult, ALU.add)
                    E.drain()
                    E.tensor_mul(st[:, zcol:zcol + 1], st[:, zcol:zcol + 1],
                                 st[:, tcol:tcol + 1])
                    E.drain()

            def vec_stats(st, xap, sqlabel):
                E = bld.eng
                if emit:
                    E.reduce_sum(st[:, 2:3], xap, axis=AXX)
                bld.wait("sS" if st is stc else "pS", bld.at(sqlabel))
                if emit:
                    E.drain()
                    E.tensor_scalar(st[:, 3:4], st[:, 2:3], 1.0 / D, None, ALU.mult)
                    E.tensor_scalar(st[:, 4:5], st[:, 1:2], 1.0 / D, EPS,
                                    ALU.mult, ALU.add)
                    E.drain()
                    E.tensor_mul(st[:, 5:6], st[:, 3:4], st[:, 3:4])
                    E.drain()
                    E.tensor_sub(st[:, 4:5], st[:, 4:5], st[:, 5:6])
                    E.drain()
                    vec_rsqrt(st, 4, 6, 5)
                    E.tensor_mul(st[:, 7:8], st[:, 3:4], st[:, 6:7])
                    E.drain()
                    E.tensor_scalar(st[:, 7:8], st[:, 7:8], -1.0, None, ALU.mult)
                    E.drain()

            def b1_sync(sync):
                bld.eng = sync
                d = sync.dma_start(out=pid_sb[:], in_=pid_d[:, :]) if emit else None
                bld.inc(d, "sX", 16)
                d = sync.dma_start(out=Pd[:], in_=bdiag_d[:, :]) if emit else None
                bld.inc(d, "sX", 16)
                for ch in range(NMCH):
                    lo, hi = ch * 8, min(ch * 8 + 8, NMASK)
                    if ch >= 2:
                        bld.wait("sV", bld.at(f"v_msk{ch - 2}"))
                    d = (sync.dma_start(out=msk[:, ch % 2, 0:hi - lo, :],
                                        in_=masks_d[lo:hi, :, :].rearrange("k p t -> p k t"))
                         if emit else None)
                    bld.inc(d, "sX", 16)
                    bld.label("sX", f"x_msk{ch}")
                for t in range(_T):
                    if t >= 2:
                        bld.wait("sS", bld.at(f"s_xn{t - 2}"))
                        bld.wait("sV", bld.at(f"v_nw{t - 2}"))
                    d = (sync.dma_start(out=xs[:, t % 2, :],
                                        in_=x_d[t * 128:(t + 1) * 128, :])
                         if emit else None)
                    bld.inc(d, "sX", 16)
                    bld.label("sX", f"x_x{t}")
                    if t % 4 == 3:
                        bld.wait("sV", bld.at(f"v_kaff{t}"))
                        d = (sync.dma_start(out=qt_scr[:, t - 3:t + 1, :], in_=qt_st[:])
                             if emit else None)
                        bld.inc(d, "sQT", 16)
                        bld.label("sQT", f"qt{t}")
                bld.seminc("b1", 1)
                bld.wait("b1", 5)

            def b1_gpsimd(g):
                bld.eng = g
                d = g.dma_start(out=gb_sb[:], in_=gb_d[:, :]) if emit else None
                bld.inc(d, "pW", 16)
                for (dst, src) in ((wqk_bf, wqk_d.rearrange("(c p) d -> p c d", p=128)),
                                   (wv_bf, wv_d.rearrange("(c p) d -> p c d", p=128)),
                                   (id_bf, ident_d[:, :]),
                                   (bqk_bf, bqk_d[:, :]), (bv_bf, bv_d[:, :])):
                    d = g.dma_start(out=dst[:], in_=src) if emit else None
                    bld.inc(d, "pW", 16)
                d = (g.dma_start(out=tbl_bc[:], in_=tbl_d[:, :].partition_broadcast(128))
                     if emit else None)
                bld.inc(d, "pW", 16)
                bld.label("pW", "w_p1")
                bld.wait("pW", bld.at("w_p1"))
                i = g.memset(ones_bf[:], 1.0) if emit else None
                bld.inc(i, "sG", 1)
                bld.label("sG", "g_pro")
                bld.seminc("b1", 1)
                bld.wait("b1", 5)

            def b1_scalar(scalar):
                bld.eng = scalar
                for t in range(_T):
                    bld.wait("sX", bld.at(f"x_x{t}"))
                    if emit:
                        scalar.activation(out=sqs[:], in_=xs[:, t % 2, :], func=AF.Square,
                                          accum_out=stc[:, 1:2])
                    bld.drain()
                    bld.nopinc("sS")
                    bld.label("sS", f"s_sq{t}")
                    bld.wait("sV", bld.at(f"v_nw{t}"))
                    if emit:
                        scalar.activation(out=xn_bf[:], in_=xs[:, t % 2, :], func=AF.Identity,
                                          bias=stc[:, 7:8], scale=stc[:, 6:7])
                    bld.drain()
                    bld.nopinc("sS")
                    bld.label("sS", f"s_xn{t}")
                    bld.wait("sT", bld.at(f"t_pj{t}"))
                    if emit:
                        scalar.activation(out=sig_v[:], in_=Vp[:], func=AF.Sigmoid)
                        scalar.activation(out=sig_qk[:], in_=QKp[:], func=AF.Sigmoid)
                    bld.drain()
                    bld.nopinc("sS")
                    bld.label("sS", f"s_sg{t}")
                bld.seminc("b1", 1)
                bld.wait("b1", 5)

            def b1_vector(vec):
                bld.eng = vec
                bld.wait("sG", bld.at("g_pro"))
                if emit:
                    vec.tensor_scalar(tbl_bc[:], tbl_bc[:], TSCALE, None, ALU.mult)
                    vec.drain()
                    vec.tensor_scalar(cst[:, 1:2], tbl_bc[:, 31:32], 1.0, GAMMA_L,
                                      ALU.mult, ALU.add)
                    vec.memset(P1[:], GAMMA_L)
                    vec.memset(cst[:, 0:1], ALPHA)
                    vec.drain()
                i = vec.engine_nop() if emit else None
                bld.inc(i, "sV")
                bld.label("sV", "v_pro")
                for ch in range(NMCH):
                    bld.wait("sX", bld.at(f"x_msk{ch}"))
                    lo, hi = ch * 8, min(ch * 8 + 8, NMASK)
                    for k in range(lo, hi):
                        pat, b = _MASK_PAIRS[k]
                        P = Pd if pat == 0 else P1
                        if emit:
                            vec.scalar_tensor_tensor(P[:], msk[:, ch % 2, k - lo, :],
                                                     tbl_bc[:, b:b + 1], P[:],
                                                     ALU.mult, ALU.add)
                            vec.drain()
                    i = vec.engine_nop() if emit else None
                    bld.inc(i, "sV")
                    bld.label("sV", f"v_msk{ch}")
                for t in range(_T):
                    bld.wait("sX", bld.at(f"x_x{t}"))
                    vec_stats(stc, xs[:, t % 2, :] if emit else None, f"s_sq{t}")
                    i = vec.engine_nop() if emit else None
                    bld.inc(i, "sV")
                    bld.label("sV", f"v_nw{t}")
                    bld.wait("sT", bld.at(f"t_tr{t}"))
                    if emit:
                        vec.tensor_copy(xnT_bf[:, t % 2, :, :],
                                        TRp.rearrange("p (c t) -> p c t", c=8)[:, :, :])
                        vec.drain()
                    i = vec.engine_nop() if emit else None
                    bld.inc(i, "sV")
                    bld.label("sV", f"v_ev{t}")
                    bld.wait("sS", bld.at(f"s_sg{t}"))
                    if emit:
                        vec.tensor_mul(v_sb[:, t, :], Vp[:], sig_v[:])
                        vec.drain()
                        vec.tensor_mul(qk_bf[:], QKp[:], sig_qk[:])
                        vec.drain()
                    i = vec.engine_nop() if emit else None
                    bld.inc(i, "sV")
                    bld.label("sV", f"v_qk{t}")
                    bld.wait("sT", bld.at(f"t_qt{t}"))
                    if t % 4 == 0 and t >= 4:
                        bld.wait("sQT", bld.at(f"qt{t - 1}"))
                    if emit:
                        vec.tensor_scalar(kT_sb[:, t * 128:(t + 1) * 128], QTp[:],
                                          gb_sb[:, 2:3], gb_sb[:, 3:4], ALU.mult, ALU.add)
                        vec.tensor_scalar(qt_st[:, t % 4, :], QTp[:],
                                          gb_sb[:, 0:1], gb_sb[:, 1:2], ALU.mult, ALU.add)
                        vec.drain()
                    i = vec.engine_nop() if emit else None
                    bld.inc(i, "sV")
                    bld.label("sV", f"v_kaff{t}")
                bld.seminc("b1", 1)
                bld.wait("b1", 5)

            def b1_tensor(ten):
                bld.eng = ten
                bld.wait("pW", bld.at("w_p1"))
                bld.wait("sG", bld.at("g_pro"))
                for t in range(_T):
                    bld.wait("sS", bld.at(f"s_xn{t}"))
                    if t >= 1:
                        bld.wait("sV", bld.at(f"v_ev{t - 1}"))
                    if emit:
                        for c in range(8):
                            ten.matmul(TRp[:, c * 128:(c + 1) * 128],
                                       xn_bf[:, c * 128:(c + 1) * 128], id_bf[:],
                                       start=True, stop=True, skip_group_check=True)
                    bld.drain()
                    bld.nopinc("sT")
                    bld.label("sT", f"t_tr{t}")
                    bld.wait("sV", bld.at(f"v_ev{t}"))
                    if t >= 1:
                        bld.wait("sV", bld.at(f"v_qk{t - 1}"))
                    if emit:
                        for half in range(2):
                            for c in range(8):
                                ten.matmul(Vp[:, half * 512:(half + 1) * 512],
                                           xnT_bf[:, t % 2, c, :],
                                           wv_bf[:, c, half * 512:(half + 1) * 512],
                                           start=(c == 0), stop=False,
                                           skip_group_check=True)
                            ten.matmul(Vp[:, half * 512:(half + 1) * 512], ones_bf[:],
                                       bv_bf[:, half * 512:(half + 1) * 512],
                                       start=False, stop=True, skip_group_check=True)
                        for c in range(8):
                            ten.matmul(QKp[:], xnT_bf[:, t % 2, c, :], wqk_bf[:, c, :],
                                       start=(c == 0), stop=False, skip_group_check=True)
                        ten.matmul(QKp[:], ones_bf[:], bqk_bf[:],
                                   start=False, stop=True, skip_group_check=True)
                    bld.drain()
                    bld.nopinc("sT")
                    bld.label("sT", f"t_pj{t}")
                    bld.wait("sV", bld.at(f"v_qk{t}"))
                    if t >= 1:
                        bld.wait("sV", bld.at(f"v_kaff{t - 1}"))
                    if emit:
                        ten.matmul(QTp[:], qk_bf[:], id_bf[:], start=True, stop=True,
                                   skip_group_check=True)
                    bld.drain()
                    bld.nopinc("sT")
                    bld.label("sT", f"t_qt{t}")
                bld.seminc("b1", 1)
                bld.wait("b1", 5)

            if emit:
                with nc.Block("p1") as blk1:
                    @blk1.sync
                    def _(sync):
                        b1_sync(sync)

                    @blk1.gpsimd
                    def _(g):
                        b1_gpsimd(g)

                    @blk1.scalar
                    def _(scalar):
                        b1_scalar(scalar)

                    @blk1.vector
                    def _(vec):
                        b1_vector(vec)

                    @blk1.tensor
                    def _(ten):
                        b1_tensor(ten)
            else:
                b1_sync(None)
                b1_gpsimd(None)
                b1_scalar(None)
                b1_vector(None)
                b1_tensor(None)

            es1.close()
            _PH = os.environ.get("KB_PHASES", "123")

            if "2" not in _PH:
                es12.close()
                return
            es2 = ExitStack()
            qt_sb = ctx(es2, nc.sbuf_tensor("qt_sb", [128, 2, 256], BF16))
            e_sb = ctx(es2, nc.sbuf_tensor("e_sb", [128, 2, 256], F32))
            z_sb = ctx(es2, nc.sbuf_tensor("z_sb", [128, 2, 256], F32))
            att_bf = ctx(es2, nc.sbuf_tensor("att_bf", [128, 2, 256], BF16))
            ao_st = ctx(es2, nc.sbuf_tensor("ao_st", [128, 2, 2048], F32))
            SIMp = [ctx(es2, nc.psum_tensor(f"SIM{i}", [128, 256], F32)) for i in range(2)]
            AOp = ctx(es2, nc.psum_tensor("AOp", [128, 2048], F32))

            def near_kind(ti0, j):
                nj = ti0 + 2
                if j == nj - 1:
                    return "d"
                if j == ti0:
                    return "b"
                if j == ti0 - 1:
                    return "c"
                return "far"

            def erf_gate(h, m, j):
                if j <= 1:
                    return bld.at(f"{h}t_simd{m}")
                return bld.at(f"{h}t_av{m}_{j - 1}")

            def p2_sync(sync, h):
                bld.eng = sync
                own = OWN[h]
                bld.wait("sQT", bld.at("qt63"))
                for m in range(16):
                    ti0 = own[2 * m]
                    if m >= 2:
                        bld.wait("pT", bld.at(f"{h}t_simlast{m - 2}"))
                    d = (sync.dma_start(out=qt_sb[:, m % 2, :],
                                        in_=qt_scr[:, ti0:ti0 + 2, :]
                                        .rearrange("p a b -> p (a b)"))
                         if emit else None)
                    bld.inc(d, "qL", 16)
                    bld.label("qL", f"{h}ql{m}")
                    if m >= 1:
                        bld.wait("pV", bld.at(f"{h}v_ao{m - 1}"))
                        d = (sync.dma_start(out=ao_scr[:, 2 * (m - 1):2 * m, :]
                                            .rearrange("p a b -> p (a b)"),
                                            in_=ao_st[:, (m - 1) % 2, :])
                             if emit else None)
                        bld.inc(d, "qSp", 16)
                        bld.label("qSp", f"{h}sp{m - 1}")
                bld.wait("pV", bld.at(f"{h}v_ao15"))
                d = (sync.dma_start(out=ao_scr[:, 30:32, :].rearrange("p a b -> p (a b)"),
                                    in_=ao_st[:, 15 % 2, :]) if emit else None)
                bld.inc(d, "qSp", 16)
                bld.label("qSp", f"{h}sp15")
                bld.wait("qSp", bld.at(f"{h}sp15"))

            def p2_tensor(ten, h):
                bld.eng = ten
                own = OWN[h]
                for m in range(16):
                    ti0 = own[2 * m]
                    nj = ti0 + 2
                    bld.wait("qL", bld.at(f"{h}ql{m}"))
                    if m >= 1:
                        bld.wait("pV", bld.at(f"{h}v_ao{m - 1}"))

                    def attnv(j):
                        bld.wait("pV", bld.at(f"{h}v_att{m}_{j}"))
                        ics = (0, 1) if j < nj - 1 else (1,)
                        mm = None
                        for ic in ics:
                            stop = (j == nj - 2) if ic == 0 else (j == nj - 1)
                            for half in range(2):
                                mm = (ten.matmul(
                                    AOp[:, (2 * ic + half) * 512:(2 * ic + half + 1) * 512],
                                    att_bf[:, j % 2, ic * 128:(ic + 1) * 128],
                                    v_sb[:, j, half * 512:(half + 1) * 512],
                                    start=(j == 0), stop=stop, skip_group_check=True)
                                    if emit else None)
                        bld.inc(mm, "pT")
                        bld.label("pT", f"{h}t_av{m}_{j}")

                    for j in range(nj):
                        if j >= 2:
                            attnv(j - 2)
                        if j >= 2:
                            bld.wait("pS", bld.at(f"{h}s_erf{m}_{j - 2}"))
                            if (j - 2) >= nj - 3:
                                bld.wait("pV", bld.at(f"{h}v_z{m}_{j - 2}"))
                        mm = (ten.matmul(SIMp[j % 2][:], kT_sb[:, j * 128:(j + 1) * 128],
                                         qt_sb[:, m % 2, :], start=True, stop=True,
                                         skip_group_check=True) if emit else None)
                        bld.inc(mm, "pT")
                        bld.label("pT", f"{h}t_sim{m}_{j}")
                        if j == 1:
                            bld.drain()
                            bld.nopinc("pT")
                            bld.label("pT", f"{h}t_simd{m}")
                    bld.label("pT", f"{h}t_simlast{m}")
                    for j in (nj - 2, nj - 1):
                        attnv(j)
                    bld.drain()
                    bld.nopinc("pT")
                    bld.label("pT", f"{h}t_ao{m}")

            def p2_scalar(scalar, h):
                bld.eng = scalar
                own = OWN[h]
                for m in range(16):
                    ti0 = own[2 * m]
                    nj = ti0 + 2
                    for j in range(nj):
                        kind = near_kind(ti0, j)
                        if kind == "far":
                            bld.wait("pT", erf_gate(h, m, j))
                            if emit:
                                scalar.activation(out=e_sb[:, j % 2, :],
                                                  in_=SIMp[j % 2][:], func=AF.Erf,
                                                  bias=cst[:, 1:2], scale=cst[:, 0:1])
                        else:
                            bld.wait("pV", bld.at(f"{h}v_z{m}_{j}"))
                            width = 128 if kind == "d" else 256
                            if emit:
                                scalar.activation(out=e_sb[:, j % 2, 0:width],
                                                  in_=z_sb[:, j % 2, 0:width],
                                                  func=AF.Erf)
                        bld.drain()
                        bld.nopinc("pS")
                        bld.label("pS", f"{h}s_erf{m}_{j}")

            def p2_vector(vec, h):
                bld.eng = vec
                own = OWN[h]
                for m in range(16):
                    ti0 = own[2 * m]
                    nj = ti0 + 2
                    for j in range(nj):
                        kind = near_kind(ti0, j)
                        if kind != "far":
                            bld.wait("pT", erf_gate(h, m, j))
                            if emit:
                                if kind == "b":
                                    vec.scalar_tensor_tensor(z_sb[:, j % 2, 0:128],
                                                             SIMp[j % 2][:, 0:128],
                                                             cst[:, 0:1], Pd[:],
                                                             ALU.mult, ALU.add)
                                    vec.scalar_tensor_tensor(z_sb[:, j % 2, 128:256],
                                                             SIMp[j % 2][:, 128:256],
                                                             cst[:, 0:1], P1[:],
                                                             ALU.mult, ALU.add)
                                elif kind == "c":
                                    vec.scalar_tensor_tensor(z_sb[:, j % 2, 0:128],
                                                             SIMp[j % 2][:, 0:128],
                                                             cst[:, 0:1], P1[:],
                                                             ALU.mult, ALU.add)
                                    vec.tensor_scalar(z_sb[:, j % 2, 128:256],
                                                      SIMp[j % 2][:, 128:256],
                                                      cst[:, 0:1], cst[:, 1:2],
                                                      ALU.mult, ALU.add)
                                else:
                                    vec.scalar_tensor_tensor(z_sb[:, j % 2, 0:128],
                                                             SIMp[j % 2][:, 128:256],
                                                             cst[:, 0:1], Pd[:],
                                                             ALU.mult, ALU.add)
                                vec.drain()
                            i = vec.engine_nop() if emit else None
                            bld.inc(i, "pV")
                            bld.label("pV", f"{h}v_z{m}_{j}")
                        bld.wait("pS", bld.at(f"{h}s_erf{m}_{j}"))
                        if j >= 2:
                            bld.wait("pT", bld.at(f"{h}t_av{m}_{j - 2}"))
                        if emit:
                            if kind == "d":
                                vec.tensor_scalar(att_bf[:, j % 2, 128:256],
                                                  e_sb[:, j % 2, 0:128], 0.5, 0.5,
                                                  ALU.mult, ALU.add)
                            else:
                                vec.tensor_scalar(att_bf[:, j % 2, :], e_sb[:, j % 2, :],
                                                  0.5, 0.5, ALU.mult, ALU.add)
                            vec.drain()
                        i = vec.engine_nop() if emit else None
                        bld.inc(i, "pV")
                        bld.label("pV", f"{h}v_att{m}_{j}")
                    bld.wait("pT", bld.at(f"{h}t_ao{m}"))
                    if m >= 2:
                        bld.wait("qSp", bld.at(f"{h}sp{m - 2}"))
                    if emit:
                        vec.tensor_copy(ao_st[:, m % 2, :], AOp[:])
                        vec.drain()
                    i = vec.engine_nop() if emit else None
                    bld.inc(i, "pV")
                    bld.label("pV", f"{h}v_ao{m}")

            def run_branch(engname, fn):
                if mode == "count":
                    before = {k: s.n for k, s in bld.sems.items()}
                    fn(None, 0)
                    after0 = {k: s.n for k, s in bld.sems.items()}
                    for k, s in bld.sems.items():
                        s.n = before[k]
                    fn(None, 1)
                    for k, s in bld.sems.items():
                        s.n = max(after0[k], s.n)
                        bld.milestones[f"{engname}_p2end_{k}"] = s.n
                else:
                    eng = bld.cur_engines[engname]
                    with eng.register(f"r_{engname}") as r, \
                            eng.register(f"rh_{engname}") as rh:
                        eng.reg_load(r, pid_sb[:1, :1])
                        eng.reg_mod(rh, r, 2)
                        before = {k: s.n for k, s in bld.sems.items()}
                        with eng.If_eq(rh, 0):
                            fn(eng, 0)
                            bld.eng = eng
                            for k, s in bld.sems.items():
                                tgt = bld.milestones[f"{engname}_p2end_{k}"]
                                if s.n < tgt:
                                    bld.seminc(k, tgt - s.n)
                        after0 = {k: s.n for k, s in bld.sems.items()}
                        for k, s in bld.sems.items():
                            s.n = before[k]
                        with eng.Else():
                            fn(eng, 1)
                            bld.eng = eng
                            for k, s in bld.sems.items():
                                tgt = bld.milestones[f"{engname}_p2end_{k}"]
                                if s.n < tgt:
                                    bld.seminc(k, tgt - s.n)
                        for k, s in bld.sems.items():
                            assert s.n == after0[k], (engname, k, s.n, after0[k])

            if emit:
                bld.cur_engines = {}
                with nc.Block("p2") as blk2:
                    @blk2.sync
                    def _(sync):
                        bld.cur_engines["sync"] = sync
                        run_branch("sync", p2_sync)
                        bld.eng = sync
                        bld.seminc("b2", 1)
                        bld.wait("b2", 4)

                    @blk2.tensor
                    def _(ten):
                        bld.cur_engines["tensor"] = ten
                        run_branch("tensor", p2_tensor)
                        bld.eng = ten
                        bld.drain()
                        bld.seminc("b2", 1)
                        bld.wait("b2", 4)

                    @blk2.scalar
                    def _(scalar):
                        bld.cur_engines["scalar"] = scalar
                        run_branch("scalar", p2_scalar)
                        bld.eng = scalar
                        bld.drain()
                        bld.seminc("b2", 1)
                        bld.wait("b2", 4)

                    @blk2.vector
                    def _(vec):
                        bld.cur_engines["vector"] = vec
                        run_branch("vector", p2_vector)
                        bld.eng = vec
                        bld.drain()
                        bld.seminc("b2", 1)
                        bld.wait("b2", 4)
            else:
                run_branch("sync", p2_sync)
                bld.seminc("b2", 1)
                run_branch("tensor", p2_tensor)
                bld.seminc("b2", 1)
                run_branch("scalar", p2_scalar)
                bld.seminc("b2", 1)
                run_branch("vector", p2_vector)
                bld.seminc("b2", 1)

            es2.close()
            es12.close()
            if "3" not in _PH:
                return

            es3 = ExitStack()
            wr_bf = ctx(es3, nc.sbuf_tensor("wr_bf", [128, 8, D], BF16))
            wk_bf = ctx(es3, nc.sbuf_tensor("wk_bf", [128, 8, D], BF16))
            wvt_bf = ctx(es3, nc.sbuf_tensor("wvt_bf", [128, 8, D], BF16))
            xp = ctx(es3, nc.sbuf_tensor("xp", [128, 2, D], F32))
            at = ctx(es3, nc.sbuf_tensor("at", [128, 2, D], F32))
            sq3 = ctx(es3, nc.sbuf_tensor("sq3", [128, D], F32))
            st3 = ctx(es3, nc.sbuf_tensor("st3", [128, 16], F32))
            xn3 = ctx(es3, nc.sbuf_tensor("xn3", [128, D], BF16))
            win = ctx(es3, nc.sbuf_tensor("win", [128, 8, 256], BF16))
            d_bf = ctx(es3, nc.sbuf_tensor("d_bf", [128, 8, 128], BF16))
            xr_bf = ctx(es3, nc.sbuf_tensor("xr_bf", [128, 8, 128], BF16))
            xk_bf = ctx(es3, nc.sbuf_tensor("xk_bf", [128, 8, 128], BF16))
            mur_bf = ctx(es3, nc.sbuf_tensor("mur_bf", [128, 8], BF16))
            muk_bf = ctx(es3, nc.sbuf_tensor("muk_bf", [128, 8], BF16))
            r_sb = ctx(es3, nc.sbuf_tensor("r_sb", [128, D], F32))
            krl_bf = ctx(es3, nc.sbuf_tensor("krl_bf", [128, 8, 128], BF16))
            kact_bf = ctx(es3, nc.sbuf_tensor("kact_bf", [128, 8, 128], BF16))
            kactT_bf = ctx(es3, nc.sbuf_tensor("kactT_bf", [128, 8, 128], BF16))
            out_sb = ctx(es3, nc.sbuf_tensor("out_sb", [128, 2, D], F32))
            TR3 = ctx(es3, nc.psum_tensor("TR3", [128, 1024], F32))
            Rp = ctx(es3, nc.psum_tensor("Rp", [128, 1024], F32))
            Kp = ctx(es3, nc.psum_tensor("Kp", [128, 1024], F32))
            KVp = ctx(es3, nc.psum_tensor("KVp", [128, 1024], F32))

            steps = []
            for ri in range(2):
                steps.append(("pre", ri))
                for oo in range(16):
                    steps.append(("tile", ri * 16 + oo))

            def p3_sync(sync):
                bld.eng = sync
                for si, (kind, o) in enumerate(steps):
                    if si >= 2:
                        pk, po = steps[si - 2]
                        bld.wait("pS", bld.at(f"3s_xn{si - 2}"))
                        bld.wait("pV", bld.at(f"3v_nw{si - 2}"))
                        if pk == "tile":
                            bld.wait("pV", bld.at(f"3v_out{po}"))
                    src = (xpre_d[o, :, :] if kind == "pre"
                           else xo_d[o * 128:(o + 1) * 128, :])
                    d = sync.dma_start(out=xp[:, si % 2, :], in_=src) if emit else None
                    bld.inc(d, "pX", 16)
                    bld.label("pX", f"3x{si}")
                    if kind == "tile":
                        if o >= 2:
                            bld.wait("pV", bld.at(f"3v_out{o - 2}"))
                        d = (sync.dma_start(out=at[:, o % 2, :], in_=ao_scr[:, o, :])
                             if emit else None)
                        bld.inc(d, "pA", 16)
                        bld.label("pA", f"3a{o}")

            def p3_gpsimd(g):
                bld.eng = g
                bld.wait("b2", 4)
                for (dst, src) in ((wr_bf, wrT_d.rearrange("(c p) d -> p c d", p=128)),
                                   (wk_bf, wkT_d.rearrange("(c p) d -> p c d", p=128)),
                                   (wvt_bf, wvT_d.rearrange("(c p) d -> p c d", p=128)),
                                   (mur_bf, mur_d[:, :]), (muk_bf, muk_d[:, :])):
                    d = g.dma_start(out=dst[:], in_=src) if emit else None
                    bld.inc(d, "pW", 16)
                bld.label("pW", "w_p3")
                for si, (kind, o) in enumerate(steps):
                    if kind != "tile":
                        continue
                    bld.wait("pV", bld.at(f"3v_out{o}"))
                    d = (g.dma_start(out=out_d[:, o, :], in_=out_sb[:, o % 2, :])
                         if emit else None)
                    bld.inc(d, "pO", 16)
                    bld.label("pO", f"3o{o}")
                bld.wait("pO", bld.at("3o31"))

            def p3_scalar(scalar):
                bld.eng = scalar
                for si, (kind, o) in enumerate(steps):
                    bld.wait("pX", bld.at(f"3x{si}"))
                    if emit:
                        scalar.activation(out=sq3[:], in_=xp[:, si % 2, :], func=AF.Square,
                                          accum_out=st3[:, 1:2])
                    bld.drain()
                    bld.nopinc("pS")
                    bld.label("pS", f"3s_sq{si}")
                    bld.wait("pV", bld.at(f"3v_nw{si}"))
                    if emit:
                        scalar.activation(out=xn3[:], in_=xp[:, si % 2, :], func=AF.Identity,
                                          bias=st3[:, 7:8], scale=st3[:, 6:7])
                    bld.drain()
                    bld.nopinc("pS")
                    bld.label("pS", f"3s_xn{si}")
                    if kind != "tile":
                        continue
                    bld.wait("pT", bld.at(f"3t_r{o}"))
                    if emit:
                        scalar.activation(out=r_sb[:], in_=Rp[:], func=AF.Sigmoid)
                    bld.drain()
                    bld.nopinc("pS")
                    bld.label("pS", f"3s_r{o}")
                    bld.wait("pT", bld.at(f"3t_k{o}"))
                    if emit:
                        scalar.activation(out=krl_bf.rearrange("p c t -> p (c t)")[:, :],
                                          in_=Kp[:], func=AF.Relu)
                    bld.drain()
                    bld.nopinc("pS")
                    bld.label("pS", f"3s_k{o}")

            def p3_vector(vec):
                bld.eng = vec
                for si, (kind, o) in enumerate(steps):
                    bld.wait("pX", bld.at(f"3x{si}"))
                    vec_stats(st3, xp[:, si % 2, :] if emit else None, f"3s_sq{si}")
                    i = vec.engine_nop() if emit else None
                    bld.inc(i, "pV")
                    bld.label("pV", f"3v_nw{si}")
                    bld.wait("pT", bld.at(f"3t_tr{si}"))
                    if emit:
                        vec.tensor_copy(win[:, :, 128:256],
                                        TR3.rearrange("p (c t) -> p c t", c=8)[:, :, :])
                        vec.drain()
                    i = vec.engine_nop() if emit else None
                    bld.inc(i, "pV")
                    bld.label("pV", f"3v_ev{si}")
                    if kind == "pre":
                        if emit:
                            vec.tensor_copy(win[:, :, 127:128], win[:, :, 255:256])
                            vec.drain()
                        i = vec.engine_nop() if emit else None
                        bld.inc(i, "pV")
                        bld.label("pV", f"3v_bl{si}")
                        continue
                    if emit:
                        vec.tensor_sub(d_bf[:], win[:, :, 128:256], win[:, :, 127:255])
                        vec.drain()
                        vec.tensor_mul(xr_bf[:], d_bf[:],
                                       mur_bf[:, :].unsqueeze(2).broadcast_to([128, 8, 128]))
                        vec.tensor_mul(xk_bf[:], d_bf[:],
                                       muk_bf[:, :].unsqueeze(2).broadcast_to([128, 8, 128]))
                        vec.drain()
                        vec.tensor_add(xr_bf[:], xr_bf[:], win[:, :, 127:255])
                        vec.tensor_add(xk_bf[:], xk_bf[:], win[:, :, 127:255])
                        vec.drain()
                        vec.tensor_copy(win[:, :, 127:128], win[:, :, 255:256])
                        vec.drain()
                    i = vec.engine_nop() if emit else None
                    bld.inc(i, "pV")
                    bld.label("pV", f"3v_bl{si}")
                    bld.wait("pS", bld.at(f"3s_k{o}"))
                    if emit:
                        vec.tensor_mul(kact_bf[:], krl_bf[:], krl_bf[:])
                        vec.drain()
                    i = vec.engine_nop() if emit else None
                    bld.inc(i, "pV")
                    bld.label("pV", f"3v_ka{o}")
                    bld.wait("pT", bld.at(f"3t_kt{o}"))
                    if emit:
                        vec.tensor_copy(kactT_bf[:],
                                        TR3.rearrange("p (c t) -> p c t", c=8)[:, :, :])
                        vec.drain()
                    i = vec.engine_nop() if emit else None
                    bld.inc(i, "pV")
                    bld.label("pV", f"3v_kt{o}")
                    bld.wait("pT", bld.at(f"3t_kv{o}"))
                    bld.wait("pS", bld.at(f"3s_r{o}"))
                    bld.wait("pA", bld.at(f"3a{o}"))
                    if o >= 2:
                        bld.wait("pO", bld.at(f"3o{o - 2}"))
                    if emit:
                        vec.tensor_mul(out_sb[:, o % 2, :], r_sb[:], KVp[:])
                        vec.drain()
                        vec.tensor_add(out_sb[:, o % 2, :], out_sb[:, o % 2, :],
                                       xp[:, si % 2, :])
                        vec.drain()
                        vec.tensor_add(out_sb[:, o % 2, :], out_sb[:, o % 2, :],
                                       at[:, o % 2, :])
                        vec.drain()
                    i = vec.engine_nop() if emit else None
                    bld.inc(i, "pV")
                    bld.label("pV", f"3v_out{o}")

            def p3_tensor(ten):
                bld.eng = ten
                bld.wait("pW", bld.at("w_p3"))
                for si, (kind, o) in enumerate(steps):
                    bld.wait("pS", bld.at(f"3s_xn{si}"))
                    if si >= 1:
                        bld.wait("pV", bld.at(f"3v_ev{si - 1}"))
                        pk, po = steps[si - 1]
                        if pk == "tile":
                            bld.wait("pV", bld.at(f"3v_kt{po}"))
                    if emit:
                        for c in range(8):
                            ten.matmul(TR3[:, c * 128:(c + 1) * 128],
                                       xn3[:, c * 128:(c + 1) * 128], id_bf[:],
                                       start=True, stop=True, skip_group_check=True)
                    bld.drain()
                    bld.nopinc("pT")
                    bld.label("pT", f"3t_tr{si}")
                    if kind != "tile":
                        continue
                    bld.wait("pV", bld.at(f"3v_bl{si}"))
                    if o >= 1:
                        bld.wait("pS", bld.at(f"3s_r{o - 1}"))
                    if emit:
                        for half in range(2):
                            for c in range(8):
                                ten.matmul(Rp[:, half * 512:(half + 1) * 512],
                                           xr_bf[:, c, :],
                                           wr_bf[:, c, half * 512:(half + 1) * 512],
                                           start=(c == 0), stop=(c == 7),
                                           skip_group_check=True)
                    bld.drain()
                    bld.nopinc("pT")
                    bld.label("pT", f"3t_r{o}")
                    if o >= 1:
                        bld.wait("pS", bld.at(f"3s_k{o - 1}"))
                    if emit:
                        for half in range(2):
                            for c in range(8):
                                ten.matmul(Kp[:, half * 512:(half + 1) * 512],
                                           xk_bf[:, c, :],
                                           wk_bf[:, c, half * 512:(half + 1) * 512],
                                           start=(c == 0), stop=(c == 7),
                                           skip_group_check=True)
                    bld.drain()
                    bld.nopinc("pT")
                    bld.label("pT", f"3t_k{o}")
                    bld.wait("pV", bld.at(f"3v_ka{o}"))
                    if emit:
                        for c in range(8):
                            ten.matmul(TR3[:, c * 128:(c + 1) * 128],
                                       kact_bf[:, c, :], id_bf[:],
                                       start=True, stop=True, skip_group_check=True)
                    bld.drain()
                    bld.nopinc("pT")
                    bld.label("pT", f"3t_kt{o}")
                    bld.wait("pV", bld.at(f"3v_kt{o}"))
                    if o >= 1:
                        bld.wait("pV", bld.at(f"3v_out{o - 1}"))
                    if emit:
                        for half in range(2):
                            for c in range(8):
                                ten.matmul(KVp[:, half * 512:(half + 1) * 512],
                                           kactT_bf[:, c, :],
                                           wvt_bf[:, c, half * 512:(half + 1) * 512],
                                           start=(c == 0), stop=(c == 7),
                                           skip_group_check=True)
                    bld.drain()
                    bld.nopinc("pT")
                    bld.label("pT", f"3t_kv{o}")

            if emit:
                with nc.Block("p3") as blk3:
                    @blk3.sync
                    def _(sync):
                        p3_sync(sync)

                    @blk3.gpsimd
                    def _(g):
                        p3_gpsimd(g)

                    @blk3.scalar
                    def _(scalar):
                        p3_scalar(scalar)

                    @blk3.vector
                    def _(vec):
                        p3_vector(vec)

                    @blk3.tensor
                    def _(ten):
                        p3_tensor(ten)
            else:
                p3_sync(None)
                p3_gpsimd(None)
                p3_scalar(None)
                p3_vector(None)
                p3_tensor(None)

            es3.close()

        gen_all("count")
        gen_all("emit")

    return nc


_NC_CACHE = None


def kernel(L0_attn_x, L0_attn_wqk, L0_attn_bqk, L0_attn_gamma, L0_attn_beta,
           L0_attn_wv, L0_attn_bv, L0_attn_rel_bias, L0_ffn_W_r, L0_ffn_W_k,
           L0_ffn_W_v, L0_ffn_mu_r, L0_ffn_mu_k):
    global _NC_CACHE
    x = np.ascontiguousarray(np.asarray(L0_attn_x, dtype=np.float32))
    wqk = np.ascontiguousarray(np.asarray(L0_attn_wqk, dtype=np.float32))
    bqk = np.ascontiguousarray(np.asarray(L0_attn_bqk, dtype=np.float32).reshape(1, DQK))
    gamma = np.asarray(L0_attn_gamma, dtype=np.float32)
    beta = np.asarray(L0_attn_beta, dtype=np.float32)
    wv = np.ascontiguousarray(np.asarray(L0_attn_wv, dtype=np.float32))
    bv = np.ascontiguousarray(np.asarray(L0_attn_bv, dtype=np.float32).reshape(1, D))
    tbl = np.ascontiguousarray(np.asarray(L0_attn_rel_bias, dtype=np.float32).reshape(1, 32))
    wrT = np.ascontiguousarray(np.asarray(L0_ffn_W_r, dtype=np.float32).T)
    wkT = np.ascontiguousarray(np.asarray(L0_ffn_W_k, dtype=np.float32).T)
    wvT = np.ascontiguousarray(np.asarray(L0_ffn_W_v, dtype=np.float32).T)
    mur = np.ascontiguousarray(np.asarray(L0_ffn_mu_r, dtype=np.float32).reshape(8, 128).T)
    muk = np.ascontiguousarray(np.asarray(L0_ffn_mu_k, dtype=np.float32).reshape(8, 128).T)
    gb = np.ascontiguousarray(
        np.stack([gamma[0], beta[0], gamma[1], beta[1]], axis=1).astype(np.float32))
    ident = np.eye(128, dtype=np.float32)

    if _NC_CACHE is None:
        _NC_CACHE = build_program()
    nc = _NC_CACHE

    shared = {
        "wqk": wqk, "bqk": bqk, "gb": gb, "wv": wv, "bv": bv, "tbl": tbl,
        "wrT": wrT, "wkT": wkT, "wvT": wvT, "mur": mur, "muk": muk,
        "ident": ident, "masks": _MASKS, "bdiag": _BASE_DIAG,
    }
    in_maps = []
    for c in range(8):
        b, h = c // 2, c % 2
        own = OWN[h]
        xb = x[b]
        xo = np.ascontiguousarray(xb.reshape(T, 128, D)[own].reshape(NOWN * 128, D))
        pres = []
        for p in PRE[h]:
            pres.append(np.zeros((128, D), np.float32) if p is None
                        else xb[p * 128:(p + 1) * 128])
        xpre = np.ascontiguousarray(np.stack(pres))
        m = dict(shared)
        m["x"] = xb
        m["xo"] = xo
        m["xpre"] = xpre
        in_maps.append(m)

    res = run_bass_kernel_spmd(nc, in_maps, core_ids=list(range(8)))

    out = np.empty((B, S, D), dtype=np.float32)
    for c in range(8):
        b, h = c // 2, c % 2
        own = OWN[h]
        o = np.asarray(res.results[c]["out"]).transpose(1, 0, 2)
        for oi, t in enumerate(own):
            out[b, t * 128:(t + 1) * 128, :] = o[oi]
    return out
